# version 01 (final)
# baseline (speedup 1.0000x reference)
import numpy as np
import concourse.bass as bass
import concourse.mybir as mybir
from concourse.bass_utils import run_bass_kernel_spmd

F32 = mybir.dt.float32
BF16 = mybir.dt.bfloat16
ALU = mybir.AluOpType
AX = mybir.AxisListType
ACT = mybir.ActivationFunctionType

NCORES = 8
S = 16384
D = 1024
NH = 16
HD = 64
DFF = 4096
TPC = S // NCORES
LN_EPS = 1e-5
MAGIC = 1.5 * (2.0 ** 23)
TWO_PI = float(2.0 * np.pi)
PI_2 = float(np.pi / 2.0)
NF = D // 128
NHF = DFF // 128
T4 = TPC // 512
DEBUG_STOP = None
RACE_DETECT = False
PARANOID = False
PARANOID_PHASES = None


class Prog:

    ENGS = ("sync", "scalar", "vector", "tensor", "gpsimd")

    def __init__(self, nc):
        self.nc = nc
        self.q = {e: [] for e in self.ENGS}
        self.sems = {}
        self.last_op = {}

    def _sem(self, name):
        if name not in self.sems:
            self.sems[name] = [self.nc.alloc_semaphore(name), 0]
        return self.sems[name]

    def op(self, eng, fn, inc=None, n=1):
        if PARANOID and (PARANOID_PHASES is None or getattr(self, "phase", None) in PARANOID_PHASES):
            snapshot = [(rec[0], rec[1]) for name, rec in self.sems.items()
                        if not name.startswith("_ctr_") or name != "_ctr_" + eng]
            ctr = self._sem("_ctr_" + eng)

            def pw(e, snapshot=snapshot):
                for h, v in snapshot:
                    if v > 0:
                        e.wait_ge(h, v)
            self.q[eng].append(pw)
            if inc is not None:
                rec = self._sem(inc)
                rec[1] += n

                def wrapped2(e, fn=fn, rec=rec, n=n):
                    r = fn(e)
                    r.then_inc(rec[0], n)
                    return r
                self.q[eng].append(wrapped2)
            else:
                ctr[1] += 1

                def wrapped3(e, fn=fn, ctr=ctr):
                    r = fn(e)
                    r.then_inc(ctr[0], 1)
                    return r
                self.q[eng].append(wrapped3)
            self.last_op[eng] = len(self.q[eng]) - 1
            return
        if inc is not None:
            rec = self._sem(inc)
            rec[1] += n

            def wrapped(e, fn=fn, rec=rec, n=n):
                fn(e).then_inc(rec[0], n)
            self.q[eng].append(wrapped)
        else:
            self.q[eng].append(fn)
        self.last_op[eng] = len(self.q[eng]) - 1

    def bump(self, eng, sem, n=1):
        if PARANOID and (PARANOID_PHASES is None or getattr(self, "phase", None) in PARANOID_PHASES):
            return
        rec = self._sem(sem)
        rec[1] += n
        idx = self.last_op.get(eng)
        assert idx is not None, f"bump on {eng} with no prior op"
        fn = self.q[eng][idx]

        def wrapped(e, fn=fn, rec=rec, n=n):
            r = fn(e)
            assert r is not None, "bump target returned no instruction"
            r.then_inc(rec[0], n)
        self.q[eng][idx] = wrapped

    def wait(self, eng, sem, val=None):
        rec = self._sem(sem)
        v = rec[1] if val is None else val
        if v <= 0:
            return

        def w(e, rec=rec, v=v):
            e.wait_ge(rec[0], v)
        self.q[eng].append(w)

    def val(self, sem):
        return self._sem(sem)[1]

    def build(self):
        nc = self.nc
        with nc.Block() as block:
            @block.sync
            def _(e):
                for fn in self.q["sync"]:
                    fn(e)

            @block.scalar
            def _(e):
                for fn in self.q["scalar"]:
                    fn(e)

            @block.vector
            def _(e):
                for fn in self.q["vector"]:
                    fn(e)

            @block.tensor
            def _(e):
                for fn in self.q["tensor"]:
                    fn(e)

            @block.gpsimd
            def _(e):
                for fn in self.q["gpsimd"]:
                    fn(e)


def build_launch_a():
    LEAN = (DEBUG_STOP == "tponly_lean")
    nc = bass.Bass("TRN2", target_bir_lowering=False, detect_race_conditions=RACE_DETECT)
    P = Prog(nc)

    x_in = nc.dram_tensor("x", [D, TPC], F32, kind="ExternalInput").ap()
    w_q = nc.dram_tensor("w_q", [D, D], F32, kind="ExternalInput").ap()
    w_k = nc.dram_tensor("w_k", [D, D], F32, kind="ExternalInput").ap()
    w_v = nc.dram_tensor("w_v", [D, D], F32, kind="ExternalInput").ap()
    w_b = nc.dram_tensor("w_b", [D, NH], F32, kind="ExternalInput").ap()
    w_g = nc.dram_tensor("w_g", [D, D], F32, kind="ExternalInput").ap()
    w_1 = nc.dram_tensor("w_1", [D, DFF], F32, kind="ExternalInput").ap()
    b_1 = nc.dram_tensor("b_1", [1, DFF], F32, kind="ExternalInput").ap()
    qph = nc.dram_tensor("qph", [1, DFF], F32, kind="ExternalInput").ap()
    qwr = nc.dram_tensor("qwr", [1, DFF], F32, kind="ExternalInput").ap()
    qwi = nc.dram_tensor("qwi", [1, DFF], F32, kind="ExternalInput").ap()
    w_2 = nc.dram_tensor("w_2", [DFF, D], F32, kind="ExternalInput").ap()
    b_2 = nc.dram_tensor("b_2", [1, D], F32, kind="ExternalInput").ap()
    o_q = nc.dram_tensor("o_q", [D, TPC], F32, kind="ExternalOutput").ap()
    o_k = nc.dram_tensor("o_k", [D, TPC], F32, kind="ExternalOutput").ap()
    o_kp = nc.dram_tensor("o_kp", [D, TPC], F32, kind="ExternalOutput").ap()
    o_vp = nc.dram_tensor("o_vp", [D, TPC], F32, kind="ExternalOutput").ap()
    o_gate = nc.dram_tensor("o_gate", [D, TPC], F32, kind="ExternalOutput").ap()
    o_stash = nc.dram_tensor("o_stash", [D, TPC], F32, kind="ExternalOutput").ap()
    blk2_in = nc.dram_tensor("blk2_in", [2, 128], F32, kind="ExternalInput").ap()
    blk16_in = nc.dram_tensor("blk16_in", [16, NF * 128], F32, kind="ExternalInput").ap()
    i32_in = nc.dram_tensor("i32_in", [32, 32], F32, kind="ExternalInput").ap()

    reg1 = nc.alloc_sbuf_tensor("reg1", [128, NF * TPC], F32)
    xf = [reg1[:, TPC * i:TPC * (i + 1)] for i in range(NF)]
    w1b = None if LEAN else [reg1[:, TPC * i:TPC * (i + 1)].bitcast(BF16) for i in range(NF)]
    xh = None if LEAN else [nc.alloc_sbuf_tensor(f"xh{i}", [128, TPC], BF16) for i in range(NF)]
    ssum = nc.alloc_sbuf_tensor("ssum", [1, TPC], F32)
    ssq = nc.alloc_sbuf_tensor("ssq", [1, TPC], F32)
    smu = ssum
    svar = ssq
    sinv = ssq
    reg2 = None if LEAN else nc.alloc_sbuf_tensor("reg2", [128, 4 * TPC], F32)
    mub = None if LEAN else reg2[:, 0:TPC]
    invb = None if LEAN else reg2[:, TPC:2 * TPC]
    stage = None if LEAN else reg2[:, 2 * TPC:3 * TPC]
    stage2 = None if LEAN else reg2[:, 3 * TPC:4 * TPC]
    actb = None if LEAN else [reg2.bitcast(BF16)[:, 512 * i:512 * (i + 1)] for i in range(NHF)]
    ones1 = nc.alloc_sbuf_tensor("ones1", [128, 1], F32)
    onesr = nc.alloc_sbuf_tensor("onesr", [1, 128], F32)
    blk2 = nc.alloc_sbuf_tensor("blk2", [2, 128], F32)
    blk2T = nc.alloc_sbuf_tensor("blk2T", [128, 2], F32)
    i32t = nc.alloc_sbuf_tensor("i32t", [32, 32], F32)
    wld = None if LEAN else [nc.alloc_sbuf_tensor(f"wld{i}", [128, DFF // 4], F32) for i in range(2)]
    wpb = None if LEAN else [nc.alloc_sbuf_tensor(f"wpb{i}", [128, D], BF16) for i in range(NF)]
    w2b = None if LEAN else [nc.alloc_sbuf_tensor(f"w2b{i}", [128, 512], BF16) for i in range(2)]
    beta = nc.alloc_sbuf_tensor("beta", [16, TPC], F32)
    nrm = nc.alloc_sbuf_tensor("nrm", [2, TPC], F32)
    blk16 = nc.alloc_sbuf_tensor("blk16", [16, NF * 128], F32)

    cvec = nc.alloc_sbuf_tensor("cvec", [32, 128], F32)
    cvec2 = nc.alloc_sbuf_tensor("cvec2", [32, 128], F32)
    sin_t = nc.alloc_sbuf_tensor("sin_t", [32, 128], F32)
    cos_t = nc.alloc_sbuf_tensor("cos_t", [32, 128], F32)
    amp_t = nc.alloc_sbuf_tensor("amp_t", [32, 128], F32)
    b1_t = nc.alloc_sbuf_tensor("b1_t", [32, 128], F32)
    qph_t = nc.alloc_sbuf_tensor("qph_t", [32, 128], F32)
    qwr_t = nc.alloc_sbuf_tensor("qwr_t", [32, 128], F32)
    qwi_t = nc.alloc_sbuf_tensor("qwi_t", [32, 128], F32)
    b2_t = nc.alloc_sbuf_tensor("b2_t", [8, 128], F32)
    colb1 = nc.alloc_sbuf_tensor("colb1", [128, NHF], F32)
    colb1q = nc.alloc_sbuf_tensor("colb1q", [128, NHF], F32)
    colb1s = nc.alloc_sbuf_tensor("colb1s", [128, NHF], F32)
    colsin = nc.alloc_sbuf_tensor("colsin", [128, NHF], F32)
    colcos = nc.alloc_sbuf_tensor("colcos", [128, NHF], F32)
    colamp = nc.alloc_sbuf_tensor("colamp", [128, NHF], F32)
    colb2 = nc.alloc_sbuf_tensor("colb2", [128, NF], F32)
    c1buf = nc.alloc_sbuf_tensor("c1buf", [128, 512], F32)
    qbuf = nc.alloc_sbuf_tensor("qbuf", [128, 512], F32)
    xre = [nc.alloc_sbuf_tensor(f"xre{i}", [128, 512], F32) for i in range(2)]
    sevac = qbuf
    eps11 = nc.alloc_sbuf_tensor("eps11", [1, 1], F32)
    c01col = nc.alloc_sbuf_tensor("c01col", [128, 1], F32)
    ps = [nc.alloc_psum_tensor(f"ps{i}", [128, 512], F32) for i in range(8)]

    P.phase = "init"
    for i in range(NF):
        P.op("sync", lambda e, i=i: e.dma_start(out=xf[i], in_=x_in[128 * i:128 * (i + 1), :]), inc="ld", n=16)
    for src, dst in [(b_1, b1_t), (qph, qph_t), (qwr, qwr_t), (qwi, qwi_t), (b_2, b2_t),
                     (blk2_in, blk2), (blk16_in, blk16), (i32_in, i32t)]:
        P.op("sync", lambda e, s=src, d=dst: e.dma_start(out=d[:], in_=s), inc="ld", n=16)
    ld_x = P.val("ld")

    P.op("vector", lambda e: e.memset(ones1[:], 1.0))
    P.op("vector", lambda e: e.memset(onesr[:], 1.0))
    P.op("vector", lambda e: e.memset(blk2T[:], 0.0))
    P.op("vector", lambda e: e.memset(blk2T[0:64, 0:1], 1.0))
    P.op("vector", lambda e: e.memset(blk2T[64:128, 1:2], 1.0))

    P.op("vector", lambda e: e.memset(eps11[:], LN_EPS))
    P.op("vector", lambda e: e.memset(c01col[:], 0.1), inc="vv")
    P.wait("scalar", "vv")

    if DEBUG_STOP == "loads":
        P.wait("vector", "ld", ld_x)
        P.wait("vector", "vv")
        P.op("vector", lambda e: e.tensor_copy(out=cvec[:], in_=qph_t[:]), inc="vv")
        P.wait("gpsimd", "vv")
        P.op("gpsimd", lambda e: e.dma_start(out=o_q[0:32, 0:128], in_=cvec[:]), inc="st", n=16)
        P.wait("gpsimd", "st")
        P.build()
        return nc

    if DEBUG_STOP in ("tponly", "tponly_lean"):
        P.wait("tensor", "ld", ld_x)
        P.op("tensor", lambda e: e.matmul(
            out=ps[5][:, 0:NHF], lhsT=b1_t[:], rhs=i32t[0:NHF, 0:NHF], start=True, stop=True))
        P.bump("tensor", "mm")
        P.wait("vector", "mm")
        P.op("vector", lambda e: e.tensor_copy(out=colb1[:, 0:NHF], in_=ps[5][:, 0:NHF]), inc="vv")
        P.wait("gpsimd", "vv")
        P.op("gpsimd", lambda e: e.dma_start(out=o_q[0:128, 0:NHF], in_=colb1[:]), inc="st", n=16)
        P.wait("gpsimd", "st")
        P.build()
        return nc

    P.phase = "consts"
    P.wait("vector", "ld", ld_x)
    P.op("vector", lambda e: e.tensor_scalar(out=cvec[:], in0=qph_t[:], scalar1=1.0 / TWO_PI, scalar2=MAGIC, op0=ALU.mult, op1=ALU.add))
    P.op("vector", lambda e: e.tensor_scalar_sub(out=cvec[:], in0=cvec[:], scalar1=MAGIC))
    P.op("vector", lambda e: e.scalar_tensor_tensor(out=cvec[:], in0=cvec[:], scalar=-TWO_PI, in1=qph_t[:], op0=ALU.mult, op1=ALU.add))
    P.op("vector", lambda e: e.tensor_scalar(out=cvec2[:], in0=qph_t[:], scalar1=1.0 / TWO_PI, scalar2=0.25, op0=ALU.mult, op1=ALU.add))
    P.op("vector", lambda e: e.tensor_scalar(out=cvec2[:], in0=cvec2[:], scalar1=MAGIC, scalar2=MAGIC, op0=ALU.add, op1=ALU.subtract))
    P.op("vector", lambda e: e.scalar_tensor_tensor(out=cvec2[:], in0=cvec2[:], scalar=-TWO_PI, in1=qph_t[:], op0=ALU.mult, op1=ALU.add))
    P.op("vector", lambda e: e.tensor_scalar_add(out=cvec2[:], in0=cvec2[:], scalar1=PI_2), inc="vv")
    if DEBUG_STOP == "rr":
        P.wait("gpsimd", "vv")
        P.op("gpsimd", lambda e: e.dma_start(out=o_q[0:32, 0:128], in_=cvec[:]), inc="st", n=16)
        P.op("gpsimd", lambda e: e.dma_start(out=o_k[0:32, 0:128], in_=cvec2[:]), inc="st", n=16)
        P.wait("gpsimd", "st")
        P.build()
        return nc

    P.wait("scalar", "vv")
    P.op("scalar", lambda e: e.activation(out=sin_t[:], in_=cvec[:], func=ACT.Sin))
    P.op("scalar", lambda e: e.activation(out=cos_t[:], in_=cvec2[:], func=ACT.Sin), inc="sc")
    P.wait("vector", "sc")
    P.op("vector", lambda e: e.tensor_tensor(out=amp_t[:], in0=cos_t[:], in1=qwr_t[:], op=ALU.mult))
    P.op("vector", lambda e: e.tensor_tensor(out=cvec[:], in0=sin_t[:], in1=qwi_t[:], op=ALU.mult))
    P.op("vector", lambda e: e.tensor_tensor(out=amp_t[:], in0=amp_t[:], in1=cvec[:], op=ALU.add))
    P.op("vector", lambda e: e.tensor_tensor(out=amp_t[:], in0=amp_t[:], in1=qwi_t[:], op=ALU.mult))
    P.op("vector", lambda e: e.tensor_scalar_mul(out=amp_t[:], in0=amp_t[:], scalar1=0.1), inc="vv")

    if DEBUG_STOP == "amp":
        P.wait("gpsimd", "vv")
        P.op("gpsimd", lambda e: e.dma_start(out=o_q[0:32, 0:128], in_=amp_t[:]), inc="st", n=16)
        P.wait("gpsimd", "st")
        P.build()
        return nc

    P.wait("tensor", "vv")
    for cols, row, n in [(colb1, b1_t, NHF), (colsin, sin_t, NHF), (colcos, cos_t, NHF),
                         (colamp, amp_t, NHF), (colb2, b2_t, NF)]:
        P.op("tensor", lambda e, row=row, n=n: e.matmul(
            out=ps[5][:, 0:n], lhsT=row[:], rhs=i32t[0:n, 0:n], start=True, stop=True))
        P.bump("tensor", "mm")
        P.wait("vector", "mm")
        P.op("vector", lambda e, cols=cols, n=n: e.tensor_copy(out=cols[:, 0:n], in_=ps[5][:, 0:n]), inc="vv")
        P.wait("tensor", "vv")
        if DEBUG_STOP == "tp1":
            P.wait("gpsimd", "vv")
            P.op("gpsimd", lambda e, cols=cols, n=n: e.dma_start(out=o_q[0:128, 0:n], in_=cols[:, 0:n]), inc="st", n=16)
            P.wait("gpsimd", "st")
            P.build()
            return nc
    P.op("vector", lambda e: e.tensor_scalar(out=colb1q[:], in0=colb1[:], scalar1=0.1, scalar2=PI_2, op0=ALU.mult, op1=ALU.add))
    P.op("vector", lambda e: e.tensor_scalar_mul(out=colb1s[:], in0=colb1[:], scalar1=0.1), inc="vv")

    if DEBUG_STOP == "consts":
        P.wait("gpsimd", "vv")
        P.op("gpsimd", lambda e: e.dma_start(out=o_q[0:128, 0:NHF], in_=colamp[:]), inc="st", n=16)
        P.op("gpsimd", lambda e: e.dma_start(out=o_k[0:128, 0:NHF], in_=colb1[:]), inc="st", n=16)
        P.wait("gpsimd", "st")
        P.build()
        return nc

    P.phase = "stats"
    P.wait("scalar", "ld", ld_x)
    for i in range(NF):
        P.op("scalar", lambda e, i=i: e.activation(out=stage, in_=xf[i], func=ACT.Square), inc="sc")
        P.wait("tensor", "sc")
        for q4 in range(T4):
            P.op("tensor", lambda e, i=i, q4=q4: e.matmul(
                out=ps[q4][0:1, :], lhsT=ones1[:], rhs=xf[i][:, 512 * q4:512 * (q4 + 1)],
                start=(i == 0), stop=(i == NF - 1)))
        for q4 in range(T4):
            P.op("tensor", lambda e, i=i, q4=q4: e.matmul(
                out=ps[4 + q4][0:1, :], lhsT=ones1[:], rhs=stage[:, 512 * q4:512 * (q4 + 1)],
                start=(i == 0), stop=(i == NF - 1)))
        P.bump("tensor", "mm")
        if i < NF - 1:
            P.wait("scalar", "mm")

    P.wait("vector", "mm")
    for q4 in range(T4):
        P.op("vector", lambda e, q4=q4: e.tensor_copy(out=ssum[0:1, 512 * q4:512 * (q4 + 1)], in_=ps[q4][0:1, :]))
        P.op("vector", lambda e, q4=q4: e.tensor_copy(out=ssq[0:1, 512 * q4:512 * (q4 + 1)], in_=ps[4 + q4][0:1, :]))
    P.op("vector", lambda e: e.tensor_scalar_mul(out=smu[:], in0=ssum[:], scalar1=1.0 / D))
    P.op("vector", lambda e: e.tensor_scalar_mul(out=svar[:], in0=ssq[:], scalar1=1.0 / D))
    P.op("vector", lambda e: e.tensor_tensor(out=mub[0:1, :], in0=smu[:], in1=smu[:], op=ALU.mult))
    P.op("vector", lambda e: e.tensor_tensor(out=svar[:], in0=svar[:], in1=mub[0:1, :], op=ALU.subtract), inc="vv")
    P.wait("scalar", "vv")
    P.op("scalar", lambda e: e.activation(out=svar[:], in_=svar[:], func=ACT.Sqrt, bias=eps11[0:1, 0:1]), inc="sc")
    P.wait("vector", "sc")
    P.op("vector", lambda e: e.reciprocal(out=sinv[:], in_=svar[:]), inc="vv")

    if DEBUG_STOP == "stats":
        P.wait("gpsimd", "vv")
        P.op("gpsimd", lambda e: e.dma_start(out=o_vp[0:1, :], in_=ssum[:]), inc="st", n=16)
        P.op("gpsimd", lambda e: e.dma_start(out=o_vp[1:2, :], in_=ssq[:]), inc="st", n=16)
        P.wait("gpsimd", "st")
        P.build()
        return nc

    for row, dstb in [(smu, mub), (sinv, invb)]:
        P.wait("tensor", "vv")
        for q4 in range(T4):
            P.op("tensor", lambda e, row=row, q4=q4: e.matmul(
                out=ps[q4][:, :], lhsT=onesr[:], rhs=row[:][0:1, 512 * q4:512 * (q4 + 1)],
                start=True, stop=True))
        P.bump("tensor", "mm")
        P.wait("vector", "mm")
        for q4 in range(T4):
            P.op("vector", lambda e, dstb=dstb, q4=q4: e.tensor_copy(
                out=dstb[:, 512 * q4:512 * (q4 + 1)], in_=ps[q4][:, :]))
        P.bump("vector", "vv")

    P.phase = "xhat"
    for i in range(NF):
        P.op("vector", lambda e, i=i: e.tensor_tensor(out=stage, in0=xf[i], in1=mub, op=ALU.subtract))
        P.op("vector", lambda e, i=i: e.tensor_tensor(out=xh[i][:], in0=stage, in1=invb, op=ALU.mult))
    P.bump("vector", "vv")
    xh_done = P.val("vv")

    if DEBUG_STOP == "xhat":
        P.op("vector", lambda e: e.tensor_copy(out=stage, in_=xh[0][:]))
        P.op("vector", lambda e: e.tensor_copy(out=stage2, in_=mub), inc="vv")
        P.wait("gpsimd", "vv")
        P.op("gpsimd", lambda e: e.dma_start(out=o_q[0:128, :], in_=stage), inc="st", n=16)
        P.op("gpsimd", lambda e: e.dma_start(out=o_k[0:128, :], in_=stage2), inc="st", n=16)
        P.op("gpsimd", lambda e: e.dma_start(out=o_kp[0:128, :], in_=invb), inc="st", n=16)
        P.op("gpsimd", lambda e: e.dma_start(out=o_vp[0:1, :], in_=ssum[:]), inc="st", n=16)
        P.op("gpsimd", lambda e: e.dma_start(out=o_vp[1:2, :], in_=ssq[:]), inc="st", n=16)
        P.wait("gpsimd", "st")
        P.build()
        return nc

    def load_w_bf16(w_ap, ncols, dst_tiles, dst_cols):
        casts = []
        P.wait("gpsimd", "vv")
        P.wait("vector", "mm")
        for kc in range(NF):
            slot = kc % 2
            buf = wld[slot]
            if kc >= 2:
                P.wait("gpsimd", "vv", casts[kc - 2])
            P.op("gpsimd", lambda e, b=buf, w=w_ap, kc=kc, m=ncols: e.dma_start(
                out=b[:, 0:m], in_=w[128 * kc:128 * (kc + 1), :]), inc=f"wl{slot}", n=16)
            P.wait("vector", f"wl{slot}")
            P.op("vector", lambda e, b=buf, kc=kc, m=dst_cols, dt=dst_tiles: e.tensor_copy(
                out=dt[kc][:, 0:m], in_=b[:, 0:m]), inc="vv")
            casts.append(P.val("vv"))
        return P.val("vv")

    P.phase = "beta"
    load_w_bf16(w_b, NH, wpb, NH)
    P.wait("tensor", "vv")
    P.wait("tensor", "vv", xh_done)
    for q4 in range(T4):
        bank = ps[6 + q4 % 2]
        if q4 >= 2:
            P.wait("tensor", "vv")
        for kc in range(NF):
            P.op("tensor", lambda e, b=bank, q4=q4, kc=kc: e.matmul(
                out=b[0:16, :], lhsT=wpb[kc][:, 0:16], rhs=xh[kc][:, 512 * q4:512 * (q4 + 1)],
                start=(kc == 0), stop=(kc == NF - 1)))
        P.bump("tensor", "mm")
        P.wait("vector", "mm")
        P.op("vector", lambda e, b=bank, q4=q4: e.tensor_copy(out=beta[:, 512 * q4:512 * (q4 + 1)], in_=b[0:16, :]), inc="vv")
    P.wait("scalar", "vv")
    P.op("scalar", lambda e: e.activation(out=beta[:], in_=beta[:], func=ACT.Sigmoid), inc="sc")
    beta_done = P.val("sc")

    def do_projection(wname, w_ap, norm, silu, store_raw, store_bmul):
        P.phase = "pw"
        load_w_bf16(w_ap, D, wpb, D)
        P.wait("tensor", "vv")
        for fb in range(NF):
            P.phase = "pmm"
            for q4 in range(T4):
                bank = ps[q4 % 2]
                for kc in range(NF):
                    P.op("tensor", lambda e, b=bank, fb=fb, q4=q4, kc=kc: e.matmul(
                        out=b[:, :], lhsT=wpb[kc][:, 128 * fb:128 * (fb + 1)],
                        rhs=xh[kc][:, 512 * q4:512 * (q4 + 1)],
                        start=(kc == 0), stop=(kc == NF - 1)))
                P.bump("tensor", "mm")
                P.wait("vector", "mm")
                if q4 == 0:
                    P.wait("vector", "st_raw")
                    P.wait("vector", "st_bm")
                P.phase = "pev"
                P.op("vector", lambda e, b=bank, q4=q4: e.tensor_copy(
                    out=stage[:, 512 * q4:512 * (q4 + 1)], in_=b[:, :]), inc="vv")
                P.phase = "pmm"
                P.wait("tensor", "vv")
            if norm:
                P.phase = "pnorm"
                P.wait("scalar", "vv")
                P.wait("scalar", "st_bm")
                P.op("scalar", lambda e: e.activation(out=stage2, in_=stage, func=ACT.Square), inc="sc")
                P.wait("tensor", "sc")
                for q4 in range(T4):
                    P.op("tensor", lambda e, q4=q4: e.matmul(
                        out=ps[2][0:2, :], lhsT=blk2T[:], rhs=stage2[:, 512 * q4:512 * (q4 + 1)],
                        start=True, stop=True))
                    P.bump("tensor", "mm")
                    P.wait("vector", "mm")
                    P.op("vector", lambda e, q4=q4: e.tensor_copy(
                        out=nrm[:, 512 * q4:512 * (q4 + 1)], in_=ps[2][0:2, :]), inc="vv")
                    P.wait("tensor", "vv")
                P.wait("scalar", "vv")
                P.op("scalar", lambda e: e.activation(out=nrm[:], in_=nrm[:], func=ACT.Sqrt), inc="sc")
                P.wait("vector", "sc")
                P.op("vector", lambda e: e.tensor_scalar_max(out=nrm[:], in0=nrm[:], scalar1=1e-12))
                P.op("vector", lambda e: e.reciprocal(out=nrm[:], in_=nrm[:]), inc="vv")
                P.wait("tensor", "vv")
                for q4 in range(T4):
                    P.op("tensor", lambda e, q4=q4: e.matmul(
                        out=ps[2][:, :], lhsT=blk2[:], rhs=nrm[:, 512 * q4:512 * (q4 + 1)],
                        start=True, stop=True))
                    P.bump("tensor", "mm")
                    P.wait("vector", "mm")
                    P.op("vector", lambda e, q4=q4: e.tensor_tensor(
                        out=stage[:, 512 * q4:512 * (q4 + 1)], in0=stage[:, 512 * q4:512 * (q4 + 1)],
                        in1=ps[2][:, :], op=ALU.mult), inc="vv")
                    P.wait("tensor", "vv")
            if silu:
                P.phase = "pst"
                P.wait("scalar", "vv")
                P.wait("scalar", "st_raw")
                P.op("scalar", lambda e: e.activation(out=stage, in_=stage, func=ACT.Silu), inc="sc")
            if store_raw is not None:
                P.phase = "pst"
                P.wait("gpsimd", "vv")
                P.wait("gpsimd", "sc")
                P.op("gpsimd", lambda e, fb=fb, o=store_raw: e.dma_start(
                    out=o[128 * fb:128 * (fb + 1), :], in_=stage), inc="st_raw", n=16)
            if store_bmul is not None:
                P.phase = "pbm"
                P.wait("tensor", "sc", beta_done)
                P.wait("tensor", "vv")
                P.wait("vector", "st_bm")
                for q4 in range(T4):
                    P.op("tensor", lambda e, fb=fb, q4=q4: e.matmul(
                        out=ps[3][:, :], lhsT=blk16[:, 128 * fb:128 * (fb + 1)],
                        rhs=beta[:, 512 * q4:512 * (q4 + 1)],
                        start=True, stop=True))
                    P.bump("tensor", "mm")
                    P.wait("vector", "mm")
                    P.op("vector", lambda e, q4=q4: e.tensor_tensor(
                        out=stage2[:, 512 * q4:512 * (q4 + 1)], in0=stage[:, 512 * q4:512 * (q4 + 1)],
                        in1=ps[3][:, :], op=ALU.mult), inc="vv")
                    P.wait("tensor", "vv")
                P.wait("gpsimd", "vv")
                P.op("gpsimd", lambda e, fb=fb, o=store_bmul: e.dma_start(
                    out=o[128 * fb:128 * (fb + 1), :], in_=stage2), inc="st_bm", n=16)
            if store_raw is not None:
                P.wait("vector", "st_raw")

    P.phase = "proj"
    do_projection("k", w_k, norm=True, silu=False, store_raw=o_k, store_bmul=o_kp)
    do_projection("v", w_v, norm=False, silu=False, store_raw=None, store_bmul=o_vp)
    do_projection("q", w_q, norm=True, silu=False, store_raw=o_q, store_bmul=None)
    do_projection("g", w_g, norm=False, silu=True, store_raw=o_gate, store_bmul=None)

    if DEBUG_STOP == "proj":
        P.wait("gpsimd", "st")
        P.build()
        return nc

    P.phase = "ffn"
    P.wait("vector", "st_raw")
    P.wait("vector", "st_bm")
    P.wait("gpsimd", "vv")
    for kc in range(NF):
        for hlf in range(4):
            slot = hlf % 2
            buf = wld[slot]
            P.op("gpsimd", lambda e, b=buf, kc=kc, hlf=hlf: e.dma_start(
                out=b[:], in_=w_1[128 * kc:128 * (kc + 1), (DFF // 4) * hlf:(DFF // 4) * (hlf + 1)]), inc=f"wl{slot}", n=16)
            P.wait("vector", f"wl{slot}")
            P.op("vector", lambda e, b=buf, kc=kc, hlf=hlf: e.tensor_copy(
                out=w1b[kc][:, (DFF // 4) * hlf:(DFF // 4) * (hlf + 1)], in_=b[:]), inc="vv")
            P.wait("gpsimd", "vv")
    P.wait("tensor", "vv")
    xre_free = [0, 0]

    n_tt = T4
    if DEBUG_STOP in ("tt1", "tt2", "tt3"):
        n_tt = int(DEBUG_STOP[2])
    for tt in range(n_tt):
        P.phase = "f1"
        for hf in range(NHF):
            for kc in range(NF):
                P.op("tensor", lambda e, tt=tt, hf=hf, kc=kc: e.matmul(
                    out=ps[4][:, :], lhsT=w1b[kc][:, 128 * hf:128 * (hf + 1)],
                    rhs=xh[kc][:, 512 * tt:512 * (tt + 1)],
                    start=(kc == 0), stop=(kc == NF - 1)))
            P.bump("tensor", "mm")
            P.wait("scalar", "mm")
            if hf > 0 or tt > 0:
                P.wait("scalar", "vv")
            P.op("scalar", lambda e, hf=hf: e.activation(out=c1buf[:], in_=ps[4][:, :], func=ACT.Sin, bias=colb1q[:, hf:hf + 1], scale=c01col[:, 0:1]), inc="sc")
            P.wait("vector", "sc")
            if hf == 0:
                P.wait("vector", "st_sev")
            P.op("vector", lambda e, hf=hf: e.tensor_scalar_mul(out=qbuf[:], in0=c1buf[:], scalar1=colsin[:, hf:hf + 1]), inc="vv")
            P.wait("scalar", "vv")
            P.op("scalar", lambda e, hf=hf: e.activation(out=c1buf[:], in_=ps[4][:, :], func=ACT.Sin, bias=colb1s[:, hf:hf + 1], scale=c01col[:, 0:1]), inc="sc")
            P.wait("vector", "sc")
            P.op("vector", lambda e, hf=hf: e.scalar_tensor_tensor(out=qbuf[:], in0=c1buf[:], scalar=colcos[:, hf:hf + 1], in1=qbuf[:], op0=ALU.mult, op1=ALU.add))
            P.op("vector", lambda e, hf=hf: e.scalar_tensor_tensor(out=qbuf[:], in0=qbuf[:], scalar=colamp[:, hf:hf + 1], in1=ps[4][:, :], op0=ALU.mult, op1=ALU.add))
            P.op("vector", lambda e, hf=hf: e.tensor_scalar(out=actb[hf], in0=qbuf[:], scalar1=colb1[:, hf:hf + 1], scalar2=0.0, op0=ALU.add, op1=ALU.max), inc="vv")
            P.wait("tensor", "vv")
        acts_done = P.val("vv")

        if DEBUG_STOP == "ffn1" and tt == 0:
            P.wait("gpsimd", "vv")
            for hh in range(8):
                P.op("gpsimd", lambda e, hh=hh: e.dma_start(
                    out=o_stash[128 * hh:128 * (hh + 1), 0:256],
                    in_=actb[hh].bitcast(F32)), inc="st", n=16)
            P.wait("gpsimd", "st")
            P.build()
            return nc

        for ofh in range(2):
            P.phase = "f2"
            P.wait("tensor", "sc")
            w2_casts = []
            w2_mms = []
            P.wait("gpsimd", "vv")
            for hf in range(NHF):
                if hf >= 2:
                    P.wait("gpsimd", "vv", w2_casts[hf - 2])
                P.op("gpsimd", lambda e, hf=hf, ofh=ofh: e.dma_start(
                    out=wld[hf % 2][:, 0:512], in_=w_2[128 * hf:128 * (hf + 1), 512 * ofh:512 * (ofh + 1)]), inc=f"wl{hf % 2}", n=16)
                P.wait("vector", f"wl{hf % 2}")
                if hf >= 2:
                    P.wait("vector", "mm", w2_mms[hf - 2])
                else:
                    P.wait("vector", "mm")
                P.op("vector", lambda e, hf=hf: e.tensor_copy(out=w2b[hf % 2][:], in_=wld[hf % 2][:, 0:512]), inc="vv")
                w2_casts.append(P.val("vv"))
                P.wait("tensor", "vv")
                if hf == 0:
                    P.wait("tensor", "vv", acts_done)
                for of in range(4):
                    P.op("tensor", lambda e, hf=hf, of=of: e.matmul(
                        out=ps[of][:, :], lhsT=w2b[hf % 2][:, 128 * of:128 * (of + 1)], rhs=actb[hf],
                        start=(hf == 0), stop=(hf == NHF - 1)))
                P.bump("tensor", "mm")
                w2_mms.append(P.val("mm"))
            of_mm = P.val("mm")
            for of in range(4):
                fb = 4 * ofh + of
                buf = of % 2
                P.wait("sync", "vv", xre_free[buf])
                P.op("sync", lambda e, fb=fb, tt=tt, buf=buf: e.dma_start(
                    out=xre[buf][:], in_=x_in[128 * fb:128 * (fb + 1), 512 * tt:512 * (tt + 1)]), inc=f"xre{buf}", n=16)
                P.wait("scalar", "mm", of_mm)
                P.wait("scalar", "st_sev")
                P.op("scalar", lambda e, of=of, fb=fb: e.activation(
                    out=sevac[:], in_=ps[of][:, :], func=ACT.Identity, bias=colb2[:, fb:fb + 1]), inc="sc")
                P.wait("vector", "sc")
                P.wait("vector", f"xre{buf}")
                P.op("vector", lambda e, buf=buf: e.tensor_tensor(out=sevac[:], in0=sevac[:], in1=xre[buf][:], op=ALU.add), inc="vv")
                xre_free[buf] = P.val("vv")
                P.wait("gpsimd", "vv")
                P.op("gpsimd", lambda e, fb=fb, tt=tt: e.dma_start(
                    out=o_stash[128 * fb:128 * (fb + 1), 512 * tt:512 * (tt + 1)], in_=sevac[:]), inc="st_sev", n=16)
                P.wait("tensor", "vv")

    for s in ("st_raw", "st_bm", "st_sev"):
        P.wait("gpsimd", s)
    P.build()
    return nc


CB = 16
SEG = CB + 1
BC = 128
NSUB = BC // CB
NBC = S // BC


def build_launch_b():
    nc = bass.Bass("TRN2", target_bir_lowering=False, detect_race_conditions=RACE_DETECT)
    P = Prog(nc)

    kj_in = nc.dram_tensor("kj", [128, S], F32, kind="ExternalInput").ap()
    qj_in = nc.dram_tensor("qj", [128, S], F32, kind="ExternalInput").ap()
    kp_in = nc.dram_tensor("kp", [128, S], F32, kind="ExternalInput").ap()
    vp_in = nc.dram_tensor("vp", [128, S], F32, kind="ExternalInput").ap()
    o_out = nc.dram_tensor("o", [128, S], F32, kind="ExternalOutput").ap()
    blk2_in = nc.dram_tensor("blk2_in", [2, 128], F32, kind="ExternalInput").ap()

    kjt = [nc.alloc_sbuf_tensor(f"kjt{i}", [128, BC], F32) for i in range(2)]
    qjt = [nc.alloc_sbuf_tensor(f"qjt{i}", [128, BC], F32) for i in range(2)]
    kpt = [nc.alloc_sbuf_tensor(f"kpt{i}", [128, BC], F32) for i in range(2)]
    vpt = [nc.alloc_sbuf_tensor(f"vpt{i}", [128, BC], F32) for i in range(2)]
    kfl = [nc.alloc_sbuf_tensor(f"kfl{i}", [2, 64 * BC], F32) for i in range(2)]
    qfl = [nc.alloc_sbuf_tensor(f"qfl{i}", [2, 64 * BC], F32) for i in range(2)]
    at = nc.alloc_sbuf_tensor("at", [128, 64 * SEG], F32)
    bt = nc.alloc_sbuf_tensor("bt", [128, 64 * SEG], F32)
    rt = nc.alloc_sbuf_tensor("rt", [128, 64 * SEG], F32)
    kkt = nc.alloc_sbuf_tensor("kkt", [128, 64 * CB], F32)
    zt = nc.alloc_sbuf_tensor("zt", [128, 64 * CB], F32)
    carry = nc.alloc_sbuf_tensor("carry", [128, 64], F32)
    zero1 = nc.alloc_sbuf_tensor("zero1", [128, 1], F32)
    blk2 = nc.alloc_sbuf_tensor("blk2", [2, 128], F32)
    ost = [nc.alloc_sbuf_tensor(f"ost{i}", [128, BC], F32) for i in range(2)]
    ps = [nc.alloc_psum_tensor(f"ps{i}", [128, 512], F32) for i in range(8)]

    def kbc(par):
        return (ps[0], ps[1]) if par == 0 else (ps[2], ps[3])

    def qbc(par):
        return (ps[4], ps[5]) if par == 0 else (ps[6], ps[7])

    P.op("sync", lambda e: e.dma_start(out=blk2[:], in_=blk2_in), inc="ldc", n=16)
    P.wait("tensor", "ldc")
    P.op("vector", lambda e: e.memset(zero1[:], 0.0))
    P.op("vector", lambda e: e.memset(carry[:], 0.0))
    P.op("vector", lambda e: e.memset(at[:], 0.0))
    P.op("vector", lambda e: e.nop(), inc="vv") if False else P.bump("vector", "vv")
    init_done = P.val("vv")
    P.wait("tensor", "vv", init_done)

    def seg_cols(t, off=1):
        return t[:].rearrange("p (j c) -> p j c", j=64)[:, :, off:off + CB]

    ld_hist = []
    fl_hist = []
    store_hist = []

    for bc in range(NBC):
        par = bc % 2
        if bc >= 2:
            P.wait("sync", "vv", ld_hist[bc - 2][1])
        for src, dst in [(kj_in, kjt[par]), (qj_in, qjt[par]), (kp_in, kpt[par]), (vp_in, vpt[par])]:
            P.op("sync", lambda e, s=src, d=dst, bc=bc: e.dma_start(
                out=d[:], in_=s[:, BC * bc:BC * (bc + 1)]), inc=f"ld{par}", n=16)
        ld_now = P.val(f"ld{par}")
        P.wait("gpsimd", f"ld{par}", ld_now)
        if bc >= 2:
            P.wait("gpsimd", "mm", fl_hist[bc - 2][1])
        for h2 in range(2):
            P.op("gpsimd", lambda e, h2=h2, par=par: e.dma_start(
                out=kfl[par][h2:h2 + 1, :], in_=kjt[par][64 * h2:64 * (h2 + 1), :]), inc=f"fl{par}", n=16)
            P.op("gpsimd", lambda e, h2=h2, par=par: e.dma_start(
                out=qfl[par][h2:h2 + 1, :], in_=qjt[par][64 * h2:64 * (h2 + 1), :]), inc=f"fl{par}", n=16)
        fl_now = P.val(f"fl{par}")

        if bc >= 2:
            P.wait("vector", f"st{par}", store_hist[bc - 2])
        for sub in range(NSUB):
            spar = sub % 2
            t0 = CB * sub
            P.wait("tensor", f"fl{par}", fl_now)
            P.wait("tensor", "vv")
            for half in range(2):
                kb = kbc(spar)[half]
                qb = qbc(spar)[half]
                P.op("tensor", lambda e, kb=kb, par=par, half=half, t0=t0: e.matmul(
                    out=kb[:, 0:512], lhsT=blk2[:],
                    rhs=kfl[par][:].rearrange("p (j c) -> p j c", j=64)[:, 32 * half:32 * (half + 1), t0:t0 + CB],
                    start=True, stop=True))
                P.op("tensor", lambda e, qb=qb, par=par, half=half, t0=t0: e.matmul(
                    out=qb[:, 0:512], lhsT=blk2[:],
                    rhs=qfl[par][:].rearrange("p (j c) -> p j c", j=64)[:, 32 * half:32 * (half + 1), t0:t0 + CB],
                    start=True, stop=True))
            P.bump("tensor", "mm")
            mm_now = P.val("mm")

            P.wait("vector", "mm", mm_now)
            P.wait("vector", f"ld{par}", ld_now)
            kpv = kpt[par][:].unsqueeze(1).broadcast_to([128, 64, BC])[:, :, t0:t0 + CB]
            vpv = vpt[par][:].unsqueeze(1).broadcast_to([128, 64, BC])[:, :, t0:t0 + CB]
            kk3 = kkt[:].rearrange("p (j c) -> p j c", j=64)
            for half in range(2):
                kb = kbc(spar)[half]
                P.op("vector", lambda e, kb=kb, half=half, kpv=kpv, kk3=kk3: e.tensor_tensor(
                    out=kk3[:, 32 * half:32 * (half + 1), :],
                    in0=kb[:, 0:512].rearrange("p (j c) -> p j c", j=32),
                    in1=kpv[:, 32 * half:32 * (half + 1), :], op=ALU.mult))
            P.bump("vector", "vv")
            P.wait("scalar", "vv")
            P.op("scalar", lambda e: e.activation(
                out=seg_cols(at), in_=kkt[:].rearrange("p (j c) -> p j c", j=64),
                func=ACT.Identity, scale=-1.0, bias=1.0), inc="sc")
            b3 = bt[:].rearrange("p (j c) -> p j c", j=64)
            for half in range(2):
                kb = kbc(spar)[half]
                P.op("vector", lambda e, kb=kb, half=half, vpv=vpv, b3=b3: e.tensor_tensor(
                    out=b3[:, 32 * half:32 * (half + 1), 1:SEG],
                    in0=kb[:, 0:512].rearrange("p (j c) -> p j c", j=32),
                    in1=vpv[:, 32 * half:32 * (half + 1), :], op=ALU.mult))
            P.op("vector", lambda e, b3=b3: e.tensor_copy(out=b3[:, :, 0:1], in_=carry[:].unsqueeze(2)), inc="vv")
            P.wait("vector", "sc")
            P.op("vector", lambda e: e.tensor_tensor_scan(
                out=rt[:], data0=at[:], data1=bt[:], initial=zero1[:],
                op0=ALU.mult, op1=ALU.add))
            r3 = rt[:].rearrange("p (j c) -> p j c", j=64)
            P.op("vector", lambda e, r3=r3: e.tensor_copy(out=carry[:].unsqueeze(2), in_=r3[:, :, SEG - 1:SEG]))
            z3 = zt[:].rearrange("p (j c) -> p j c", j=64)
            for half in range(2):
                qb = qbc(spar)[half]
                P.op("vector", lambda e, qb=qb, half=half, r3=r3, z3=z3: e.tensor_tensor(
                    out=z3[:, 32 * half:32 * (half + 1), :],
                    in0=r3[:, 32 * half:32 * (half + 1), 1:SEG],
                    in1=qb[:, 0:512].rearrange("p (j c) -> p j c", j=32), op=ALU.mult))
            P.bump("vector", "vv")
            P.op("vector", lambda e, z3=z3, sub=sub, par=par: e.tensor_reduce(
                out=ost[par][:, CB * sub:CB * (sub + 1)], in_=z3.transpose([0, 2, 1]),
                axis=AX.X, op=ALU.add), inc="vv")

        P.wait("gpsimd", "vv")
        if bc >= 2:
            pass
        P.op("gpsimd", lambda e, par=par, bc=bc: e.dma_start(
            out=o_out[:, BC * bc:BC * (bc + 1)], in_=ost[par][:]), inc=f"st{par}", n=16)
        ld_hist.append((ld_now, P.val("vv")))
        fl_hist.append((fl_now, P.val("mm")))
        store_hist.append(P.val(f"st{par}"))

    P.wait("gpsimd", "st0")
    P.wait("gpsimd", "st1")
    P.build()
    return nc


def build_launch_c():
    nc = bass.Bass("TRN2", target_bir_lowering=False, detect_race_conditions=RACE_DETECT)
    P = Prog(nc)

    o_in = nc.dram_tensor("o", [D, TPC], F32, kind="ExternalInput").ap()
    gate_in = nc.dram_tensor("gate", [D, TPC], F32, kind="ExternalInput").ap()
    stash_in = nc.dram_tensor("stash", [D, TPC], F32, kind="ExternalInput").ap()
    lng = nc.dram_tensor("lng", [1, D], F32, kind="ExternalInput").ap()
    lnb = nc.dram_tensor("lnb", [1, D], F32, kind="ExternalInput").ap()
    w_o = nc.dram_tensor("w_o", [D, D], F32, kind="ExternalInput").ap()
    out = nc.dram_tensor("out", [D, TPC], F32, kind="ExternalOutput").ap()

    of = [nc.alloc_sbuf_tensor(f"of{i}", [128, TPC], F32) for i in range(NF)]
    gb = [nc.alloc_sbuf_tensor(f"gb{i}", [128, TPC], BF16) for i in range(NF)]
    xsq = nc.alloc_sbuf_tensor("xsq", [128, TPC], F32)
    gt = nc.alloc_sbuf_tensor("gt", [128, TPC], F32)
    ssum = nc.alloc_sbuf_tensor("ssum", [1, TPC], F32)
    ssq = nc.alloc_sbuf_tensor("ssq", [1, TPC], F32)
    smu = nc.alloc_sbuf_tensor("smu", [1, TPC], F32)
    svar = nc.alloc_sbuf_tensor("svar", [1, TPC], F32)
    sinv = nc.alloc_sbuf_tensor("sinv", [1, TPC], F32)
    mub = nc.alloc_sbuf_tensor("mub", [128, TPC], F32)
    invb = nc.alloc_sbuf_tensor("invb", [128, TPC], F32)
    ones1 = nc.alloc_sbuf_tensor("ones1", [128, 1], F32)
    onesr = nc.alloc_sbuf_tensor("onesr", [1, 128], F32)
    one11 = nc.alloc_sbuf_tensor("one11", [1, 1], F32)
    colg = nc.alloc_sbuf_tensor("colg", [128, NF], F32)
    colb = nc.alloc_sbuf_tensor("colb", [128, NF], F32)
    lng_t = nc.alloc_sbuf_tensor("lng_t", [1, D], F32)
    lnb_t = nc.alloc_sbuf_tensor("lnb_t", [1, D], F32)
    wld = [nc.alloc_sbuf_tensor(f"wld{i}", [128, D], F32) for i in range(2)]
    wob = [nc.alloc_sbuf_tensor(f"wob{i}", [128, D], BF16) for i in range(NF)]
    stg = nc.alloc_sbuf_tensor("stg", [128, 512], F32)
    eps11 = nc.alloc_sbuf_tensor("eps11", [1, 1], F32)
    sre = [nc.alloc_sbuf_tensor(f"sre{i}", [128, 512], F32) for i in range(2)]
    ps = [nc.alloc_psum_tensor(f"ps{i}", [128, 512], F32) for i in range(8)]

    for i in range(NF):
        P.op("sync", lambda e, i=i: e.dma_start(out=of[i][:], in_=o_in[128 * i:128 * (i + 1), :]), inc="ld", n=16)
    P.op("sync", lambda e: e.dma_start(out=lng_t[:], in_=lng), inc="ld", n=16)
    P.op("sync", lambda e: e.dma_start(out=lnb_t[:], in_=lnb), inc="ld", n=16)
    ld_o = P.val("ld")

    P.op("vector", lambda e: e.memset(ones1[:], 1.0))
    P.op("vector", lambda e: e.memset(onesr[:], 1.0))
    P.op("vector", lambda e: e.memset(one11[:], 1.0))
    P.op("vector", lambda e: e.memset(eps11[:], LN_EPS), inc="vv")
    P.wait("scalar", "vv")

    P.wait("tensor", "ld", ld_o)
    P.wait("tensor", "vv")
    for cols, row in [(colg, lng_t), (colb, lnb_t)]:
        for c in range(NF):
            P.op("tensor", lambda e, row=row, c=c: e.matmul(
                out=ps[5][:, c:c + 1], lhsT=row[0:1, 128 * c:128 * (c + 1)], rhs=one11[:], start=True, stop=True))
        P.bump("tensor", "mm")
        P.wait("vector", "mm")
        P.op("vector", lambda e, cols=cols: e.tensor_copy(out=cols[:, 0:NF], in_=ps[5][:, 0:NF]), inc="vv")
        P.wait("tensor", "vv")

    for i in range(NF):
        for q4 in range(T4):
            P.op("tensor", lambda e, i=i, q4=q4: e.matmul(
                out=ps[q4][0:1, :], lhsT=ones1[:], rhs=of[i][:, 512 * q4:512 * (q4 + 1)],
                start=(i == 0), stop=(i == NF - 1)))
    P.bump("tensor", "mm")
    P.wait("vector", "mm")
    for q4 in range(T4):
        P.op("vector", lambda e, q4=q4: e.tensor_copy(out=ssum[0:1, 512 * q4:512 * (q4 + 1)], in_=ps[q4][0:1, :]))
    P.op("vector", lambda e: e.tensor_scalar_mul(out=smu[:], in0=ssum[:], scalar1=1.0 / D), inc="vv")
    P.wait("tensor", "vv")
    for q4 in range(T4):
        P.op("tensor", lambda e, q4=q4: e.matmul(
            out=ps[q4][:, :], lhsT=onesr[:], rhs=smu[0:1, 512 * q4:512 * (q4 + 1)], start=True, stop=True))
    P.bump("tensor", "mm")
    P.wait("vector", "mm")
    for q4 in range(T4):
        P.op("vector", lambda e, q4=q4: e.tensor_copy(
            out=mub[:, 512 * q4:512 * (q4 + 1)], in_=ps[q4][:, :]))
    P.bump("vector", "vv")
    for i in range(NF):
        P.op("vector", lambda e, i=i: e.tensor_tensor(
            out=of[i][:], in0=of[i][:], in1=mub[:], op=ALU.subtract), inc="vv")
        P.wait("scalar", "vv")
        if i > 0:
            P.wait("scalar", "mm")
        P.op("scalar", lambda e, i=i: e.activation(out=xsq[:], in_=of[i][:], func=ACT.Square), inc="sc")
        P.wait("tensor", "sc")
        for q4 in range(T4):
            P.op("tensor", lambda e, i=i, q4=q4: e.matmul(
                out=ps[4 + q4][0:1, :], lhsT=ones1[:], rhs=xsq[:, 512 * q4:512 * (q4 + 1)],
                start=(i == 0), stop=(i == NF - 1)))
        P.bump("tensor", "mm")
    P.wait("vector", "mm")
    for q4 in range(T4):
        P.op("vector", lambda e, q4=q4: e.tensor_copy(out=ssq[0:1, 512 * q4:512 * (q4 + 1)], in_=ps[4 + q4][0:1, :]))
    P.op("vector", lambda e: e.tensor_scalar_mul(out=svar[:], in0=ssq[:], scalar1=1.0 / D), inc="vv")
    P.wait("scalar", "vv")
    P.op("scalar", lambda e: e.activation(out=svar[:], in_=svar[:], func=ACT.Sqrt, bias=eps11[0:1, 0:1]), inc="sc")
    P.wait("vector", "sc")
    P.op("vector", lambda e: e.reciprocal(out=sinv[:], in_=svar[:]), inc="vv")
    P.wait("tensor", "vv")
    for q4 in range(T4):
        P.op("tensor", lambda e, q4=q4: e.matmul(
            out=ps[q4][:, :], lhsT=onesr[:], rhs=sinv[0:1, 512 * q4:512 * (q4 + 1)], start=True, stop=True))
    P.bump("tensor", "mm")
    P.wait("vector", "mm")
    for q4 in range(T4):
        P.op("vector", lambda e, q4=q4: e.tensor_copy(
            out=invb[:, 512 * q4:512 * (q4 + 1)], in_=ps[q4][:, :]))
    P.bump("vector", "vv")

    for i in range(NF):
        P.op("sync", lambda e, i=i: e.dma_start(out=gt[:], in_=gate_in[128 * i:128 * (i + 1), :]), inc="ldg", n=16)
        P.wait("vector", "ldg")
        P.op("vector", lambda e, i=i: e.tensor_tensor(out=xsq[:], in0=of[i][:], in1=invb[:], op=ALU.mult))
        P.op("vector", lambda e, i=i: e.tensor_scalar(out=xsq[:], in0=xsq[:], scalar1=colg[:, i:i + 1], scalar2=colb[:, i:i + 1], op0=ALU.mult, op1=ALU.add))
        P.op("vector", lambda e, i=i: e.tensor_tensor(out=gb[i][:], in0=xsq[:], in1=gt[:], op=ALU.mult), inc="vv")
        P.wait("sync", "vv")
    gb_done = P.val("vv")

    casts = []
    for kc in range(NF):
        slot = kc % 2
        buf = wld[slot]
        if kc >= 2:
            P.wait("gpsimd", "vv", casts[kc - 2])
        P.op("gpsimd", lambda e, b=buf, kc=kc: e.dma_start(out=b[:], in_=w_o[128 * kc:128 * (kc + 1), :]), inc=f"wl{slot}", n=16)
        P.wait("vector", f"wl{slot}")
        P.op("vector", lambda e, b=buf, kc=kc: e.tensor_copy(out=wob[kc][:], in_=b[:]), inc="vv")
        casts.append(P.val("vv"))
    P.wait("tensor", "vv")

    sre_free = [0, 0]
    for fb in range(NF):
        for q4 in range(T4):
            bank = ps[q4 % 2]
            for kc in range(NF):
                P.op("tensor", lambda e, b=bank, fb=fb, q4=q4, kc=kc: e.matmul(
                    out=b[:, :], lhsT=wob[kc][:, 128 * fb:128 * (fb + 1)],
                    rhs=gb[kc][:, 512 * q4:512 * (q4 + 1)],
                    start=(kc == 0), stop=(kc == NF - 1)))
            P.bump("tensor", "mm")
            buf = q4 % 2
            P.wait("sync", "vv", sre_free[buf])
            P.op("sync", lambda e, fb=fb, q4=q4, buf=buf: e.dma_start(
                out=sre[buf][:], in_=stash_in[128 * fb:128 * (fb + 1), 512 * q4:512 * (q4 + 1)]), inc=f"sre{buf}", n=16)
            P.wait("vector", "mm")
            P.wait("vector", f"sre{buf}")
            P.op("vector", lambda e, b=bank, buf=buf: e.tensor_tensor(out=stg[:], in0=b[:, :], in1=sre[buf][:], op=ALU.add), inc="vv")
            sre_free[buf] = P.val("vv")
            P.wait("tensor", "vv")
            P.wait("gpsimd", "vv")
            P.op("gpsimd", lambda e, fb=fb, q4=q4: e.dma_start(
                out=out[128 * fb:128 * (fb + 1), 512 * q4:512 * (q4 + 1)], in_=stg[:]), inc="st", n=16)
            P.wait("vector", "st")

    P.wait("gpsimd", "st")
    P.build()
    return nc


def _const_blk2():
    a = np.zeros((2, 128), np.float32)
    a[0, 0:64] = 1.0
    a[1, 64:128] = 1.0
    return a


def _const_blk16():
    a = np.zeros((16, NF * 128), np.float32)
    for fb in range(NF):
        a[2 * fb, 128 * fb:128 * fb + 64] = 1.0
        a[2 * fb + 1, 128 * fb + 64:128 * (fb + 1)] = 1.0
    return a


def kernel(L0_attn_x, L0_attn_Wq, L0_attn_Wk, L0_attn_Wv, L0_attn_Wbeta, L0_attn_Wg,
           L0_attn_Wout, L0_attn_ln_g, L0_attn_ln_b, L0_ffn_W1, L0_ffn_b1,
           L0_ffn_quantum_phases, L0_ffn_quantum_weights_real,
           L0_ffn_quantum_weights_imag, L0_ffn_W2, L0_ffn_b2):
    f32 = np.float32
    x = np.asarray(L0_attn_x, f32).reshape(S, D)

    nc_a = build_launch_a()
    common = {
        "w_q": np.ascontiguousarray(np.asarray(L0_attn_Wq, f32)),
        "w_k": np.ascontiguousarray(np.asarray(L0_attn_Wk, f32)),
        "w_v": np.ascontiguousarray(np.asarray(L0_attn_Wv, f32)),
        "w_b": np.ascontiguousarray(np.asarray(L0_attn_Wbeta, f32)),
        "w_g": np.ascontiguousarray(np.asarray(L0_attn_Wg, f32)),
        "w_1": np.ascontiguousarray(np.asarray(L0_ffn_W1, f32)),
        "b_1": np.asarray(L0_ffn_b1, f32).reshape(1, DFF),
        "qph": np.asarray(L0_ffn_quantum_phases, f32).reshape(1, DFF),
        "qwr": np.asarray(L0_ffn_quantum_weights_real, f32).reshape(1, DFF),
        "qwi": np.asarray(L0_ffn_quantum_weights_imag, f32).reshape(1, DFF),
        "w_2": np.ascontiguousarray(np.asarray(L0_ffn_W2, f32)),
        "b_2": np.asarray(L0_ffn_b2, f32).reshape(1, D),
        "blk2_in": _const_blk2(),
        "blk16_in": _const_blk16(),
        "i32_in": np.eye(32, dtype=np.float32),
    }
    in_maps_a = []
    for c in range(NCORES):
        xs = np.ascontiguousarray(x[TPC * c:TPC * (c + 1), :].T)
        in_maps_a.append({"x": xs, **common})
    res_a = run_bass_kernel_spmd(nc_a, in_maps_a, core_ids=list(range(NCORES)))
    ra = res_a.results

    k_all = np.concatenate([ra[c]["o_k"] for c in range(NCORES)], axis=1)
    q_all = np.concatenate([ra[c]["o_q"] for c in range(NCORES)], axis=1)
    kp_all = np.concatenate([ra[c]["o_kp"] for c in range(NCORES)], axis=1)
    vp_all = np.concatenate([ra[c]["o_vp"] for c in range(NCORES)], axis=1)

    nc_b = build_launch_b()
    in_maps_b = []
    for c in range(NCORES):
        sl = slice(128 * c, 128 * (c + 1))
        in_maps_b.append({
            "kj": np.ascontiguousarray(k_all[sl]),
            "qj": np.ascontiguousarray(q_all[sl]),
            "kp": np.ascontiguousarray(kp_all[sl]),
            "vp": np.ascontiguousarray(vp_all[sl]),
            "blk2_in": _const_blk2(),
        })
    res_b = run_bass_kernel_spmd(nc_b, in_maps_b, core_ids=list(range(NCORES)))
    rb = res_b.results

    o_all = np.concatenate([rb[c]["o"] for c in range(NCORES)], axis=0)

    nc_c = build_launch_c()
    in_maps_c = []
    for c in range(NCORES):
        sl = slice(TPC * c, TPC * (c + 1))
        in_maps_c.append({
            "o": np.ascontiguousarray(o_all[:, sl]),
            "gate": np.ascontiguousarray(ra[c]["o_gate"]),
            "stash": np.ascontiguousarray(ra[c]["o_stash"]),
            "lng": np.asarray(L0_attn_ln_g, f32).reshape(1, D),
            "lnb": np.asarray(L0_attn_ln_b, f32).reshape(1, D),
            "w_o": np.ascontiguousarray(np.asarray(L0_attn_Wout, f32)),
        })
    res_c = run_bass_kernel_spmd(nc_c, in_maps_c, core_ids=list(range(NCORES)))
    rc = res_c.results

    out = np.concatenate([rc[c]["out"].T for c in range(NCORES)], axis=0)
    return out.reshape(1, S, D).astype(np.float32)
